# version 02
# baseline (speedup 1.0000x reference)
import sys
import numpy as np
from contextlib import ExitStack

sys.path.insert(0, "/opt/trn_rl_repo")
import concourse.bass as bass
import concourse.mybir as mybir
from concourse.bass_utils import run_bass_kernel_spmd

F32 = mybir.dt.float32
BF16 = mybir.dt.bfloat16
AF = mybir.ActivationFunctionType
ALU = mybir.AluOpType
AX = mybir.AxisListType

D = 1024
NH = 16
DH = 64
NE = 8
FF = 4096
NCH = 32
OWN0 = 16
CH = 128
NK = D // 128
NFB = 8
NFC = 4
TT = 512
PHASES = "AB"
ALVL = 99

ENGINES = ("sync", "gpsimd", "tensor", "vector", "scalar")


_ENGSEM = {"vector": "sv", "scalar": "sa"}


class Sched:
    def __init__(self, sems):
        self.q = {e: [] for e in ENGINES}
        self.cnt = {}
        self.waited = {e: {} for e in ENGINES}
        self.sems = sems
        self.dirty = {}

    def op(self, eng, method, *args, inc=None, inc_n=1, **kw):
        aps = [a for a in args if hasattr(a, "tensor")]
        aps += [v for v in kw.values() if hasattr(v, "tensor")]
        for a in aps:
            w = self.dirty.get(a.tensor.name)
            if w is not None:
                self.wait(eng, w[0], w[1])
        if inc is not None:
            self.cnt[inc] = self.cnt.get(inc, 0) + inc_n
            sem, n = self.sems[inc], inc_n

            def fn(e, _m=method, _a=args, _k=kw, _s=sem, _n=n):
                getattr(e, _m)(*_a, **_k).then_inc(_s, _n)
        else:
            def fn(e, _m=method, _a=args, _k=kw):
                getattr(e, _m)(*_a, **_k)
        self.q[eng].append(fn)
        if eng in _ENGSEM:
            outa = kw.get("out")
            if outa is None and aps and args and hasattr(args[0], "tensor"):
                outa = args[0]
            if outa is not None and hasattr(outa, "tensor"):
                assert inc == _ENGSEM[eng], (
                    f"writer on {eng} must inc its sem: {method}")
                self.dirty[outa.tensor.name] = (_ENGSEM[eng], self.cnt[inc])
        return self.cnt.get(inc, 0) if inc else None

    def wait(self, eng, sem, val=None):
        if val is None:
            val = self.cnt.get(sem, 0)
        if val <= 0 or self.waited[eng].get(sem, -1) >= val:
            return
        self.waited[eng][sem] = val
        h = self.sems[sem]
        self.q[eng].append(lambda e, _h=h, _v=val: e.wait_ge(_h, _v))


def build_program():
    nc = bass.Bass("TRN2", target_bir_lowering=False)
    n_own = NCH - OWN0
    TT_ = min(TT, n_own * CH)
    NTT = (n_own * CH) // TT_
    ntok = TT_ // 128

    xpad = nc.dram_tensor("xpad", [NCH * CH, D], F32, kind="ExternalInput").ap()
    wq = nc.dram_tensor("wq", [D, D], F32, kind="ExternalInput").ap()
    wk = nc.dram_tensor("wk", [D, D], F32, kind="ExternalInput").ap()
    wv = nc.dram_tensor("wv", [D, D], F32, kind="ExternalInput").ap()
    wout = nc.dram_tensor("wout", [D, D], F32, kind="ExternalInput").ap()
    wbeta = nc.dram_tensor("wbeta", [D, NH], F32, kind="ExternalInput").ap()
    bbeta = nc.dram_tensor("bbeta", [NH], F32, kind="ExternalInput").ap()
    wgate = nc.dram_tensor("wgate", [D, NE], F32, kind="ExternalInput").ap()
    wgu = nc.dram_tensor("wgu", [NE, D, 2 * FF], F32, kind="ExternalInput").ap()
    wdn = nc.dram_tensor("wdn", [NE, FF, D], F32, kind="ExternalInput").ap()
    cident = nc.dram_tensor("cident", [128, 128], F32, kind="ExternalInput").ap()
    cslmask = nc.dram_tensor("cslmask", [128, 128], F32, kind="ExternalInput").ap()
    csumask = nc.dram_tensor("csumask", [128, 128], F32, kind="ExternalInput").ap()
    out = nc.dram_tensor("out", [n_own * CH, D], F32, kind="ExternalOutput").ap()

    with ExitStack() as ctx:
        sb = lambda n, s, d: ctx.enter_context(nc.sbuf_tensor(n, s, d))
        psd = lambda n, s, d: ctx.enter_context(nc.psum_tensor(n, s, d))

        SHARED = 101248
        shared = sb("shared", [128, SHARED], mybir.dt.uint8)
        shared_base = nc.lookup_mloc(shared).addr
        _roff = [0]

        def _ralloc(name, shape, dtype):
            per = 1
            for d_ in shape[1:]:
                per *= d_
            per *= mybir.dt.size(dtype)
            h = nc.alloc_sbuf_tensor_at(name, shape, dtype,
                                        offset=shared_base + _roff[0])
            _roff[0] = (_roff[0] + per + 31) // 32 * 32
            assert _roff[0] <= SHARED, f"shared region overflow at {name}"
            return h

        def region_reset():
            _roff[0] = 0

        xnT_own = sb("xnT_own", [128, NK * n_own * CH], BF16)
        acc = sb("acc", [128, n_own * D], F32)
        gates = sb("gates", [128, n_own * NE], F32)
        M32 = sb("M32", [64, NH * DH], F32)
        M16 = sb("M16", [64, NH * DH], BF16)
        M16n = sb("M16n", [64, NH * DH], BF16)
        identB = sb("identB", [128, 128], BF16)
        identF = sb("identF", [128, 128], F32)
        slmask = sb("slmask", [128, 128], F32)
        sumask = sb("sumask", [128, 128], F32)


        WqB = _ralloc("WqB", [128, NK * D], BF16)
        WkB = _ralloc("WkB", [128, NK * D], BF16)
        WvB = _ralloc("WvB", [128, NK * D], BF16)
        WoB = _ralloc("WoB", [128, NK * D], BF16)
        WbB = _ralloc("WbB", [128, NK * NH], BF16)
        WgB = _ralloc("WgB", [128, NK * NE], BF16)
        bbn = _ralloc("bbn", [16, 1], F32)
        bbp = _ralloc("bbp", [16, 1], F32)
        xin0 = _ralloc("xin0", [128, D], F32)
        xin = [xin0, xin0]
        xnb = _ralloc("xnb", [128, D], BF16)
        xnT_pad = _ralloc("xnT_pad", [128, NK * CH], BF16)
        KT = _ralloc("KTb", [128, 8 * 128], BF16)
        VT = _ralloc("VTb", [128, 8 * 128], BF16)
        Vtb = _ralloc("Vtb", [128, 8 * DH], BF16)
        Rt = Vtb
        QT = _ralloc("QTb", [128, 8 * 128], BF16)
        betaT = _ralloc("betaT", [16, 128], F32)
        bcols = _ralloc("bcols", [128, 16], F32)
        zgs = _ralloc("zgs", [8, 128], F32)
        zt = _ralloc("ztb", [128, 8], F32)
        Ktok = _ralloc("Ktok", [128, 8 * DH], BF16)
        Kn = Ktok
        KnT = _ralloc("KnTb", [64, 8 * 128], BF16)
        Qh0 = _ralloc("Qh0", [64, 8 * 128], BF16)
        ssb = _ralloc("ssb", [128, 8], F32)
        invb = _ralloc("invb", [128, 8], F32)
        Gb = _ralloc("Gb", [128, 8 * 128], BF16)
        Gp = _ralloc("Gpb", [128, 8 * 128], BF16)
        Pk = Gb
        Pq = Gp
        Sa = _ralloc("Sab", [128, 8 * 128], BF16)
        Sb_ = Sa
        Xg = _ralloc("Xgb", [128, 8 * DH], BF16)
        Lg = _ralloc("Lgb", [128, 8 * 128], BF16)
        Oall = _ralloc("Oallb", [128, 8 * DH], BF16)
        Qtok = Oall
        OTb = _ralloc("OTb", [128, 8 * 128], BF16)
        c_s1 = _ralloc("c_s1", [128, 1], F32)
        c_mean = _ralloc("c_mean", [128, 1], F32)
        c_s2 = _ralloc("c_s2", [128, 1], F32)
        c_m2 = _ralloc("c_m2", [128, 1], F32)
        c_var = _ralloc("c_var", [128, 1], F32)
        c_rstd = _ralloc("c_rstd", [128, 1], F32)
        g_nm = _ralloc("g_nm", [128, 1], F32)
        g_ex = _ralloc("g_ex", [128, 8], F32)
        g_sum = _ralloc("g_sum", [128, 1], F32)
        g_rs = _ralloc("g_rs", [128, 1], F32)
        g_p = _ralloc("g_p", [128, 8], F32)
        g_m1 = _ralloc("g_m1", [128, 1], F32)
        g_m2 = _ralloc("g_m2", [128, 1], F32)
        g_mv = _ralloc("g_mv", [128, 8], F32)
        g_pm1 = _ralloc("g_pm1", [128, 8], F32)
        g_pm2 = _ralloc("g_pm2", [128, 8], F32)
        g_ne = _ralloc("g_ne", [128, 8], F32)
        g_den = _ralloc("g_den", [128, 1], F32)

        region_reset()
        WguB = [_ralloc("WguB0", [128, NK * 1024], BF16),
                _ralloc("WguB1", [128, NK * 1024], BF16)]
        WdnB = [_ralloc("WdnB0", [128, NFC * 1024], BF16),
                _ralloc("WdnB1", [128, NFC * 1024], BF16)]
        actb = _ralloc("actb", [128, NFC * 512], BF16)
        sgb = [_ralloc("sgb0", [128, 512], F32), _ralloc("sgb1", [128, 512], F32)]

        psA = psd("psA", [128, 512], F32)
        psB = psd("psB", [128, 512], F32)
        psC = psd("psC", [128, 512], F32)
        psD = psd("psD", [128, 512], F32)
        psW = psd("psW", [128, 512], F32)
        psV = psd("psV", [128, 512], F32)
        psT = psd("psT", [128, 1024], BF16)
        psU = psd("psU", [128, 1024], BF16)

        sems = {}
        for name in ("sc", "si", "sxe", "sxo", "swp", "sw", "so", "sp", "sv", "sa"):
            sems[name] = ctx.enter_context(nc.semaphore(name))
        S = Sched(sems)
        block = ctx.enter_context(nc.Block())

        op, wait = S.op, S.wait

        def pewait():
            wait("tensor", "sv")
            wait("tensor", "sa")

        def vop(method, *args, sync=False, **kw):
            op("vector", method, *args, inc="sv", **kw)
            if sync:
                wait("vector", "sv")

        def aop(method, *args, **kw):
            op("scalar", method, *args, inc="sa", **kw)

        op("sync", "dma_start", out=identF[:], in_=cident, inc="sc", inc_n=16)
        op("sync", "dma_start", out=slmask[:], in_=cslmask, inc="sc", inc_n=16)
        op("sync", "dma_start", out=sumask[:], in_=csumask, inc="sc", inc_n=16)
        op("sync", "dma_start", out=bbp[:],
           in_=bbeta.rearrange("(p o) -> p o", o=1), inc="sc", inc_n=16)
        op("sync", "dma_start",
           out=acc[:].rearrange("p (i d) -> p i d", d=D),
           in_=xpad[OWN0 * CH:NCH * CH, :].rearrange("(i p) d -> p i d", p=128),
           inc="si", inc_n=16)

        op("gpsimd", "dma_start", out=identB[:], in_=cident, inc="swp", inc_n=16)
        for wsrc, wdst in ((wq, WqB), (wk, WkB), (wv, WvB), (wout, WoB)):
            for k in range(NK):
                op("gpsimd", "dma_start", out=wdst[:, k * D:(k + 1) * D],
                   in_=wsrc[k * 128:(k + 1) * 128, :], inc="swp", inc_n=16)
        op("gpsimd", "dma_start",
           out=WbB[:].rearrange("p (k h) -> p k h", k=NK),
           in_=wbeta.rearrange("(k p) h -> p k h", p=128), inc="swp", inc_n=16)
        op("gpsimd", "dma_start",
           out=WgB[:].rearrange("p (k h) -> p k h", k=NK),
           in_=wgate.rearrange("(k p) h -> p k h", p=128), inc="swp", inc_n=16)
        proj_w = S.cnt["swp"]

        wait("vector", "sc")
        vop("tensor_scalar_mul", bbn[:], bbp[:], -1.0)
        vop("memset", M32[:], 0.0)
        vop("memset", M16[:], 0.0)
        vop("memset", M16n[:], 0.0)

        do_A = "A" in PHASES
        wait("tensor", "sc")
        wait("tensor", "swp", proj_w)
        for c in (range(NCH) if do_A else []):
            own = c >= OWN0
            tc = c - OWN0
            xb = xin[c % 2]

            if c >= 1:
                wait("sync", "sv")
            sxp = "sxe" if c % 2 == 0 else "sxo"
            op("sync", "dma_start", out=xb[:], in_=xpad[c * CH:(c + 1) * CH, :],
               inc=sxp, inc_n=16)

            wait("vector", sxp)
            vop("reduce_sum", c_s1[:], xb[:], axis=AX.X, sync=True)
            vop("tensor_scalar_mul", c_mean[:], c_s1[:], 1.0 / D)
            vop("tensor_mul", xnb[:], xb[:], xb[:])
            vop("reduce_sum", c_s2[:], xnb[:], axis=AX.X, sync=True)
            vop("tensor_mul", c_m2[:], c_mean[:], c_mean[:], sync=True)
            vop("scalar_tensor_tensor", c_var[:], c_s2[:], 1.0 / D, c_m2[:],
                ALU.mult, ALU.subtract, sync=True)
            vop("tensor_scalar_add", c_var[:], c_var[:], 1e-5, sync=True)
            wait("scalar", "sv")
            aop("activation", c_rstd[:], c_var[:], AF.Ln)
            aop("activation", c_rstd[:], c_rstd[:], AF.Exp, scale=-0.5)
            wait("vector", "sa")
            vop("tensor_scalar", xnb[:], xb[:], c_mean[:], c_rstd[:],
                ALU.subtract, ALU.mult, sync=True)

            pewait()
            for i in range(NK):
                op("tensor", "transpose", psT[:, i * 128:(i + 1) * 128],
                   in_=xnb[:, i * 128:(i + 1) * 128], identity=identB[:],
                   inc=("sp" if i == NK - 1 else None))
            wait("vector", "sp")
            if own:
                xd = xnT_own[:].rearrange("p (k t) -> p k t", k=NK)[
                    :, :, tc * CH:(tc + 1) * CH]
            else:
                xd = xnT_pad[:].rearrange("p (k t) -> p k t", k=NK)
            vop("tensor_copy", xd, psT[:].rearrange("p (k t) -> p k t", k=NK))

            def XN(k):
                if own:
                    base = k * (n_own * CH) + tc * CH
                    return xnT_own[:, base:base + CH]
                return xnT_pad[:, k * CH:(k + 1) * CH]

            wait("tensor", "swp", proj_w)

            def proj(wbuf, dst):
                for half, bank in ((0, psA), (1, psB)):
                    pewait()
                    for i in range(4):
                        ii = half * 4 + i
                        for k in range(NK):
                            op("tensor", "matmul", bank[:, i * 128:(i + 1) * 128],
                               lhsT=wbuf[:, k * D + ii * 128:k * D + (ii + 1) * 128],
                               rhs=XN(k), start=(k == 0), stop=(k == NK - 1),
                               inc=("sp" if (i == 3 and k == NK - 1) else None))
                    wait("vector", "sp")
                    vop("tensor_copy", dst[:, half * 512:(half + 1) * 512], bank[:])

            if ALVL >= 2:
                proj(WkB, KT)
                proj(WvB, VT)
                if own:
                    proj(WqB, QT)

            if ALVL < 3:
                continue
            pewait()
            for k in range(NK):
                op("tensor", "matmul", psW[0:16, 0:128],
                   lhsT=WbB[:, k * NH:(k + 1) * NH], rhs=XN(k),
                   start=(k == 0), stop=(k == NK - 1),
                   inc=("sp" if k == NK - 1 else None))
            wait("scalar", "sp")
            aop("activation", betaT[:], psW[0:16, 0:128], AF.Exp,
                scale=-1.0, bias=bbn[:])
            wait("vector", "sa")
            vop("tensor_scalar_add", betaT[:], betaT[:], 1.0)
            vop("reciprocal", betaT[:], betaT[:], sync=True)
            pewait()
            op("tensor", "transpose", psC[:, 0:16], in_=betaT[:],
               identity=identF[0:16, 0:16], inc="sp")
            wait("vector", "sp")
            vop("tensor_copy", bcols[:], psC[:, 0:16])

            if own and ALVL >= 4:
                pewait()
                for k in range(NK):
                    op("tensor", "matmul", psW[0:8, 128:256],
                       lhsT=WgB[:, k * NE:(k + 1) * NE], rhs=XN(k),
                       start=(k == 0), stop=(k == NK - 1),
                       inc=("sp" if k == NK - 1 else None))
                wait("vector", "sp")
                vop("tensor_copy", zgs[:], psW[0:8, 128:256], sync=True)
                pewait()
                op("tensor", "transpose", psC[:, 16:24], in_=zgs[:],
                   identity=identF[0:8, 0:8], inc="sp")
                wait("vector", "sp")
                vop("tensor_copy", zt[:], psC[:, 16:24], sync=True)
                vop("reduce_max", g_nm[:], zt[:], axis=AX.X, negate=True, sync=True)
                wait("scalar", "sv")
                aop("activation", g_ex[:], zt[:], AF.Exp, bias=g_nm[:])
                wait("vector", "sa")
                vop("reduce_sum", g_sum[:], g_ex[:], axis=AX.X, sync=True)
                vop("reciprocal", g_rs[:], g_sum[:], sync=True)
                vop("tensor_scalar_mul", g_p[:], g_ex[:], g_rs[:], sync=True)
                vop("reduce_max", g_m1[:], g_p[:], axis=AX.X, sync=True)
                vop("memset", g_mv[:], -1e30)
                vop("tensor_copy", g_mv[:, 0:1], g_m1[:], sync=True)
                vop("match_replace", g_pm1[:], in_to_replace=g_mv[:],
                    in_values=g_p[:], imm_value=-1.0, sync=True)
                vop("reduce_max", g_m2[:], g_pm1[:], axis=AX.X, sync=True)
                vop("tensor_copy", g_mv[:, 0:1], g_m2[:], sync=True)
                vop("match_replace", g_pm2[:], in_to_replace=g_mv[:],
                    in_values=g_pm1[:], imm_value=-1.0, sync=True)
                vop("tensor_tensor", g_ne[:], g_p[:], g_pm2[:], ALU.not_equal)
                vop("tensor_add", g_den[:], g_m1[:], g_m2[:], sync=True)
                vop("tensor_scalar_max", g_den[:], g_den[:], 1e-9, sync=True)
                vop("reciprocal", g_den[:], g_den[:], sync=True)
                vop("tensor_mul", g_ne[:], g_ne[:], g_p[:], sync=True)
                vop("tensor_scalar", gates[:, tc * NE:(tc + 1) * NE],
                    g_ne[:], g_den[:], None, ALU.mult)

            for g in (range(2) if ALVL >= 5 else []):
                def PT(buf, p):
                    t = g * 4 + p
                    return buf[:, t * 128:(t + 1) * 128]

                def TH(buf, hl, local=False):
                    t = (0 if local else g * 4) + hl // 2
                    b = 64 * (hl % 2)
                    return buf[b:b + 64, t * 128:(t + 1) * 128]

                def BC(hl):
                    j = g * 8 + hl
                    return bcols[:, j:j + 1]

                pewait()
                for p in range(4):
                    op("tensor", "transpose", psT[:, p * 128:(p + 1) * 128],
                       in_=PT(KT, p), identity=identB[:],
                       inc=("sp" if p == 3 else None))
                wait("vector", "sp")
                vop("tensor_copy", Ktok[:], psT[:, 0:512])

                wait("vector", "sa")
                wait("vector", "sp")
                vop("tensor_mul", psW[:], Ktok[:], Ktok[:], sync=True)
                vop("reduce_sum", ssb[:],
                    psW[:].rearrange("p (h d) -> p h d", h=8), axis=AX.X, sync=True)
                vop("tensor_scalar_max", ssb[:], ssb[:], 1e-24, sync=True)
                wait("scalar", "sv")
                aop("activation", invb[:], ssb[:], AF.Ln)
                aop("activation", invb[:], invb[:], AF.Exp, scale=-0.5)
                wait("vector", "sa")
                for hl in range(8):
                    vop("tensor_scalar_mul", Kn[:, hl * DH:(hl + 1) * DH],
                        Ktok[:, hl * DH:(hl + 1) * DH], invb[:, hl:hl + 1],
                        sync=(hl == 7))

                pewait()
                for hl in range(8):
                    op("tensor", "transpose", psT[0:64, hl * 128:(hl + 1) * 128],
                       in_=Kn[:, hl * DH:(hl + 1) * DH], identity=identB[:],
                       inc=("sp" if hl == 7 else None))
                wait("vector", "sp")
                vop("tensor_copy", KnT[:], psT[0:64, :])

                def F32T(hl):
                    bank = psA if hl < 4 else psB
                    s = hl % 4
                    return bank[:, s * 128:(s + 1) * 128]

                def F32T2(hl):
                    bank = psC if hl < 4 else psD
                    s = hl % 4
                    return bank[:, s * 128:(s + 1) * 128]

                if ALVL < 6:
                    continue
                pewait()
                for p in range(4):
                    op("tensor", "transpose", psT[:, p * 128:(p + 1) * 128],
                       in_=PT(VT, p), identity=identB[:],
                       inc=("sp" if p == 3 else None))
                wait("vector", "sp")
                for hl in range(8):
                    pp = hl // 2
                    b = 64 * (hl % 2)
                    vop("tensor_scalar_mul", Vtb[:, hl * DH:(hl + 1) * DH],
                        psT[:, pp * 128 + b:pp * 128 + b + 64], BC(hl))
                if ALVL < 6.05:
                    for hl in range(8):
                        vop("tensor_copy", Rt[:, hl * DH:(hl + 1) * DH],
                            Vtb[:, hl * DH:(hl + 1) * DH])
                    continue
                pewait()
                for hl in range(8):
                    hg = g * 8 + hl
                    op("tensor", "matmul", psV[:, hl * DH:(hl + 1) * DH],
                       lhsT=KnT[0:64, hl * 128:(hl + 1) * 128],
                       rhs=M16n[0:64, hg * DH:(hg + 1) * DH],
                       start=True, stop=True,
                       inc=("sp" if hl == 7 else None))
                wait("vector", "sp")
                if ALVL < 6.08:
                    for hl in range(8):
                        vop("tensor_copy", Rt[:, hl * DH:(hl + 1) * DH],
                            psV[:, hl * DH:(hl + 1) * DH])
                    continue
                for hl in range(8):
                    vop("scalar_tensor_tensor", Rt[:, hl * DH:(hl + 1) * DH],
                        psV[:, hl * DH:(hl + 1) * DH], BC(hl),
                        Vtb[:, hl * DH:(hl + 1) * DH], ALU.mult, ALU.add)

                if ALVL < 6.2:
                    continue
                pewait()
                for hl in range(8):
                    op("tensor", "matmul", F32T(hl),
                       lhsT=KnT[0:64, hl * 128:(hl + 1) * 128],
                       rhs=KnT[0:64, hl * 128:(hl + 1) * 128], start=True, stop=True,
                       inc=("sp" if hl == 7 else None))
                wait("vector", "sp")
                for hl in range(8):
                    vop("scalar_tensor_tensor", Gb[:, hl * 128:(hl + 1) * 128],
                        F32T(hl), BC(hl), slmask[:],
                        ALU.mult, ALU.mult)

                if ALVL < 6.4:
                    continue
                pewait()
                for hl in range(8):
                    op("tensor", "transpose", psT[:, hl * 128:(hl + 1) * 128],
                       in_=Gb[:, hl * 128:(hl + 1) * 128], identity=identB[:],
                       inc=("sp" if hl == 7 else None))
                wait("vector", "sp")
                vop("tensor_copy", Gp[:], psT[:])
                for hl in range(8):
                    vop("tensor_sub", Sa[:, hl * 128:(hl + 1) * 128], identB[:],
                        Gp[:, hl * 128:(hl + 1) * 128], sync=(hl == 7))

                if ALVL < 7:
                    continue
                scur, snext = Sa, Sb_
                for lev in range(1, 7):
                    pewait()
                    for hl in range(8):
                        sl_ = slice(hl * 128, (hl + 1) * 128)
                        lp = Gb[:, sl_] if lev == 1 else Pk[:, sl_]
                        lq = Gp[:, sl_] if lev == 1 else Pq[:, sl_]
                        op("tensor", "matmul", F32T(hl), lhsT=lq, rhs=lp,
                           start=True, stop=True,
                           inc=("sp" if (lev == 6 and hl == 7) else None))
                        if lev < 6:
                            op("tensor", "matmul", F32T2(hl), lhsT=lp, rhs=lq,
                               start=True, stop=True,
                               inc=("sp" if hl == 7 else None))
                    wait("vector", "sp")
                    wait("scalar", "sp")
                    vop("tensor_copy", Pk[:, 0:512], psA[:])
                    vop("tensor_copy", Pk[:, 512:1024], psB[:])
                    if lev < 6:
                        aop("copy", Pq[:, 0:512], psC[:])
                        aop("copy", Pq[:, 512:1024], psD[:])
                    pewait()
                    for hl in range(8):
                        sl_ = slice(hl * 128, (hl + 1) * 128)
                        op("tensor", "matmul", F32T(hl), lhsT=Pk[:, sl_],
                           rhs=scur[:, sl_], start=True, stop=False)
                        op("tensor", "matmul", F32T(hl), lhsT=identB[:],
                           rhs=scur[:, sl_], start=False, stop=True,
                           inc=("sp" if hl == 7 else None))
                    wait("vector", "sp")
                    vop("tensor_copy", snext[:, 0:512], psA[:])
                    vop("tensor_copy", snext[:, 512:1024], psB[:])
                    scur, snext = snext, scur

                if ALVL < 8:
                    continue
                pewait()
                for hl in range(8):
                    op("tensor", "matmul", psV[:, hl * DH:(hl + 1) * DH],
                       lhsT=scur[:, hl * 128:(hl + 1) * 128],
                       rhs=Rt[:, hl * DH:(hl + 1) * DH], start=True, stop=True,
                       inc=("sp" if hl == 7 else None))
                wait("vector", "sp")
                vop("tensor_copy", Xg[:], psV[:])

                pewait()
                for hl in range(8):
                    op("tensor", "matmul",
                       psW[0:64, hl * DH:(hl + 1) * DH],
                       lhsT=Kn[:, hl * DH:(hl + 1) * DH],
                       rhs=Xg[:, hl * DH:(hl + 1) * DH], start=True, stop=True,
                       inc=("sp" if hl == 7 else None))
                wait("vector", "sp")
                for hl in range(8):
                    hg = g * 8 + hl
                    vop("tensor_tensor", M32[:, hg * DH:(hg + 1) * DH],
                        M32[:, hg * DH:(hg + 1) * DH],
                        psW[0:64, hl * DH:(hl + 1) * DH], ALU.add,
                        sync=(hl == 7))

                if own and ALVL >= 9:
                    pewait()
                    for p in range(4):
                        op("tensor", "transpose", psT[:, p * 128:(p + 1) * 128],
                           in_=PT(QT, p), identity=identB[:],
                           inc=("sp" if p == 3 else None))
                    wait("vector", "sp")
                    vop("tensor_copy", Qtok[:], psT[:, 0:512], sync=True)
                    pewait()
                    for hl in range(8):
                        op("tensor", "transpose", psT[0:64, hl * 128:(hl + 1) * 128],
                           in_=Qtok[:, hl * DH:(hl + 1) * DH], identity=identB[:],
                           inc=("sp" if hl == 7 else None))
                    wait("vector", "sp")
                    vop("tensor_copy", Qh0[:], psT[0:64, :], sync=True)
                    pewait()
                    for hl in range(8):
                        op("tensor", "matmul", F32T(hl),
                           lhsT=KnT[0:64, hl * 128:(hl + 1) * 128],
                           rhs=Qh0[0:64, hl * 128:(hl + 1) * 128], start=True, stop=True,
                           inc=("sp" if hl == 7 else None))
                    wait("vector", "sp")
                    for hl in range(8):
                        vop("tensor_mul", Lg[:, hl * 128:(hl + 1) * 128],
                            F32T(hl), sumask[:],
                            sync=(hl == 7))
                    pewait()
                    for hl in range(8):
                        hg = g * 8 + hl
                        op("tensor", "matmul", psW[:, hl * DH:(hl + 1) * DH],
                           lhsT=Qh0[0:64, hl * 128:(hl + 1) * 128],
                           rhs=M16[0:64, hg * DH:(hg + 1) * DH],
                           start=True, stop=False)
                        op("tensor", "matmul", psW[:, hl * DH:(hl + 1) * DH],
                           lhsT=Lg[:, hl * 128:(hl + 1) * 128],
                           rhs=Xg[:, hl * DH:(hl + 1) * DH],
                           start=False, stop=True, inc=("sp" if hl == 7 else None))
                    wait("vector", "sp")
                    vop("tensor_copy", Oall[:], psW[:], sync=True)
                    pewait()
                    for p in range(4):
                        op("tensor", "transpose", psT[:, p * 128:(p + 1) * 128],
                           in_=Oall[:, p * 128:(p + 1) * 128], identity=identB[:],
                           inc=("sp" if p == 3 else None))
                    wait("vector", "sp")
                    vop("tensor_copy", OTb[:, g * 512:(g + 1) * 512], psT[:, 0:512])

            vop("tensor_copy", M16[:], M32[:])
            vop("tensor_scalar_mul", M16n[:], M32[:], -1.0, sync=True)

            if own and ALVL >= 10:
                pewait()
                for i in range(8):
                    bank = psC if i < 4 else psD
                    for k in range(NK):
                        op("tensor", "matmul",
                           bank[:, (i % 4) * 128:((i % 4) + 1) * 128],
                           lhsT=WoB[:, k * D + i * 128:k * D + (i + 1) * 128],
                           rhs=OTb[:, k * 128:(k + 1) * 128],
                           start=(k == 0), stop=(k == NK - 1),
                           inc=("sp" if (i == 7 and k == NK - 1) else None))
                wait("vector", "sp")
                vop("tensor_copy", Lg[:, 0:512], psC[:])
                vop("tensor_copy", Lg[:, 512:1024], psD[:], sync=True)
                pewait()
                for i in range(8):
                    op("tensor", "transpose", psT[:, i * 128:(i + 1) * 128],
                       in_=Lg[:, i * 128:(i + 1) * 128], identity=identB[:],
                       inc=("sp" if i == 7 else None))
                wait("vector", "sp")
                wait("vector", "si", 16)
                vop("tensor_tensor", acc[:, tc * D:(tc + 1) * D],
                    acc[:, tc * D:(tc + 1) * D], psT[:], ALU.add)

        pewait()
        pa_sp = S.cnt.get("sp", 0)
        wait("gpsimd", "sp", pa_sp)
        wait("gpsimd", "sv")
        wait("gpsimd", "sa")
        if "B" not in PHASES:
            NE_eff = 0
        else:
            NE_eff = NE
        wgu5 = wgu.rearrange("e r (h f c) -> e f r h c", h=2, f=NFB)
        wdn4 = wdn.rearrange("e (f p) d -> e p f d", p=128)

        wg_free = None
        wg_free = [pa_sp, pa_sp]
        gfree = {id(psA): 0, id(psC): 0}
        ufree = {id(psB): 0, id(psD): 0}
        dfree = {id(psW): 0, id(psV): 0}
        it = 0
        for e_i in range(NE_eff):
            for fb in range(NFB):
                par = it % 2
                wait("gpsimd", "sp", wg_free[par])
                wait("gpsimd", "sw")
                for k in range(NK):
                    op("gpsimd", "dma_start",
                       out=WguB[par][:, k * 1024:(k + 1) * 1024].rearrange(
                           "p (h c) -> p h c", h=2),
                       in_=wgu5[e_i, fb, k * 128:(k + 1) * 128, :, :],
                       inc="sw", inc_n=16)
                op("gpsimd", "dma_start",
                   out=WdnB[par][:].rearrange("p (f d) -> p f d", f=NFC),
                   in_=wdn4[e_i, :, fb * NFC:(fb + 1) * NFC, :],
                   inc="sw", inc_n=16)
                w_cnt = S.cnt["sw"]

                wait("tensor", "sw", w_cnt)
                for tt in range(NTT):
                    for fc in range(NFC):
                        gbank = psA if fc % 2 == 0 else psC
                        ubank = psB if fc % 2 == 0 else psD
                        wait("tensor", "sa", gfree[id(gbank)])
                        wait("tensor", "sv", ufree[id(ubank)])
                        for k in range(NK):
                            op("tensor", "matmul", gbank[:, 0:TT_],
                               lhsT=WguB[par][:, k * 1024 + fc * 128:
                                              k * 1024 + (fc + 1) * 128],
                               rhs=xnT_own[:, k * (n_own * CH) + tt * TT_:
                                           k * (n_own * CH) + (tt + 1) * TT_],
                               start=(k == 0), stop=(k == NK - 1))
                        for k in range(NK):
                            op("tensor", "matmul", ubank[:, 0:TT_],
                               lhsT=WguB[par][:, k * 1024 + 512 + fc * 128:
                                              k * 1024 + 512 + (fc + 1) * 128],
                               rhs=xnT_own[:, k * (n_own * CH) + tt * TT_:
                                           k * (n_own * CH) + (tt + 1) * TT_],
                               start=(k == 0), stop=(k == NK - 1),
                               inc=("sp" if k == NK - 1 else None))
                        wait("scalar", "sp")
                        sg = sgb[fc % 2]
                        aop("activation", sg[:, 0:TT_], gbank[:, 0:TT_], AF.Silu)
                        gfree[id(gbank)] = S.cnt["sa"]
                        wait("vector", "sa")
                        vop("tensor_mul", actb[:, fc * TT_:(fc + 1) * TT_],
                            sg[:, 0:TT_], ubank[:, 0:TT_])
                        ufree[id(ubank)] = S.cnt["sv"]
                    act_ready = S.cnt["sv"]
                    wait("tensor", "sv", act_ready)
                    for gi in range(2 * ntok):
                        tok, dh = gi // 2, gi % 2
                        dbank = psW if gi % 2 == 0 else psV
                        wait("tensor", "sv", dfree[id(dbank)])
                        for fc in range(NFC):
                            op("tensor", "matmul", dbank[:],
                               lhsT=actb[:, fc * TT_ + tok * 128:
                                         fc * TT_ + (tok + 1) * 128],
                               rhs=WdnB[par][:, fc * 1024 + dh * 512:
                                             fc * 1024 + (dh + 1) * 512],
                               start=(fc == 0), stop=(fc == NFC - 1),
                               inc=("sp" if fc == NFC - 1 else None))
                        wait("vector", "sp")
                        tc2 = tt * ntok + tok
                        vop("scalar_tensor_tensor",
                            acc[:, tc2 * D + dh * 512:tc2 * D + (dh + 1) * 512],
                            dbank[:], gates[:, tc2 * NE + e_i:tc2 * NE + e_i + 1],
                            acc[:, tc2 * D + dh * 512:tc2 * D + (dh + 1) * 512],
                            ALU.mult, ALU.add)
                        dfree[id(dbank)] = S.cnt["sv"]
                wg_free[par] = S.cnt["sp"]
                it += 1

        wait("sync", "sv")
        op("sync", "dma_start", out=out.rearrange("(i p) d -> p i d", p=128),
           in_=acc[:].rearrange("p (i d) -> p i d", d=D), inc="so", inc_n=16)
        wait("sync", "so")
        wait("gpsimd", "so")

        @block.sync
        def _(e):
            for fn in S.q["sync"]:
                fn(e)

        @block.gpsimd
        def _(e):
            for fn in S.q["gpsimd"]:
                fn(e)

        @block.tensor
        def _(e):
            for fn in S.q["tensor"]:
                fn(e)

        @block.vector
        def _(e):
            for fn in S.q["vector"]:
                fn(e)

        @block.scalar
        def _(e):
            for fn in S.q["scalar"]:
                fn(e)

    return nc


_NC_CACHE = None


def kernel(L0_attn_x, L0_attn_Wq, L0_attn_Wk, L0_attn_Wv, L0_attn_Wbeta,
           L0_attn_b_beta, L0_attn_Wout, L0_ffn_w_gate, L0_ffn_w_gate_up,
           L0_ffn_w_down):
    global _NC_CACHE
    x = np.ascontiguousarray(np.asarray(L0_attn_x, dtype=np.float32))
    common = dict(
        wq=np.ascontiguousarray(L0_attn_Wq, dtype=np.float32),
        wk=np.ascontiguousarray(L0_attn_Wk, dtype=np.float32),
        wv=np.ascontiguousarray(L0_attn_Wv, dtype=np.float32),
        wout=np.ascontiguousarray(L0_attn_Wout, dtype=np.float32),
        wbeta=np.ascontiguousarray(L0_attn_Wbeta, dtype=np.float32),
        bbeta=np.ascontiguousarray(L0_attn_b_beta, dtype=np.float32),
        wgate=np.ascontiguousarray(L0_ffn_w_gate, dtype=np.float32),
        wgu=np.ascontiguousarray(L0_ffn_w_gate_up, dtype=np.float32),
        wdn=np.ascontiguousarray(L0_ffn_w_down, dtype=np.float32),
        cident=np.eye(128, dtype=np.float32),
        cslmask=np.tril(np.ones((128, 128), np.float32), -1),
        csumask=np.triu(np.ones((128, 128), np.float32), 0),
    )
    in_maps = []
    for c in range(8):
        b, h = c >> 1, c & 1
        if h == 0:
            xs = np.concatenate([np.zeros((2048, D), np.float32), x[b, 0:2048]], 0)
        else:
            xs = x[b]
        m = dict(common)
        m["xpad"] = np.ascontiguousarray(xs)
        in_maps.append(m)

    if _NC_CACHE is None:
        _NC_CACHE = build_program()
    res = run_bass_kernel_spmd(_NC_CACHE, in_maps, core_ids=list(range(8)))
    outp = np.empty((4, 4096, D), dtype=np.float32)
    for c in range(8):
        b, h = c >> 1, c & 1
        outp[b, h * 2048:(h + 1) * 2048, :] = res.results[c]["out"]
    return outp


# version 03
# speedup vs baseline: 1.2494x; 1.2494x over previous
import sys
import numpy as np
from contextlib import ExitStack

sys.path.insert(0, "/opt/trn_rl_repo")
import concourse.bass as bass
import concourse.mybir as mybir
from concourse.bass_utils import run_bass_kernel_spmd

F32 = mybir.dt.float32
BF16 = mybir.dt.bfloat16
AF = mybir.ActivationFunctionType
ALU = mybir.AluOpType
AX = mybir.AxisListType

D = 1024
NH = 16
DH = 64
NE = 8
FF = 4096
NCH = 32
OWN0 = 16
CH = 128
NK = D // 128
NFB = 8
NFC = 4
TT = 512
PHASES = "AB"

ENGINES = ("sync", "gpsimd", "tensor", "vector", "scalar")
_ENGSEM = {"vector": "sv", "scalar": "sa"}
_PSUM = {"psA", "psB", "psC", "psD", "psW", "psV", "psT", "psU"}


class Sched:

    def __init__(self, sems):
        self.q = {e: [] for e in ENGINES}
        self.cnt = {}
        self.waited = {e: {} for e in ENGINES}
        self.sems = sems
        self.dirty = {}
        self.readers = {}
        self.psum_ready = {}
        self.pe_pending_w = set()
        self.pe_pending_r = set()

    def _mark(self, table, nm, sem, c):
        d = table.setdefault(nm, {})
        if d.get(sem, -1) < c:
            d[sem] = c

    def op(self, eng, method, *args, inc=None, inc_n=1, **kw):
        aps = [a for a in args if hasattr(a, "tensor")]
        aps += [v for v in kw.values() if hasattr(v, "tensor")]
        outa = kw.get("out")
        if outa is None and args and hasattr(args[0], "tensor"):
            outa = args[0]
        names = [(a.tensor.name, a is outa) for a in aps]
        for nm, _ in names:
            for sem, c in self.dirty.get(nm, {}).items():
                self.wait(eng, sem, c)
            if nm in _PSUM and eng != "tensor":
                pr = self.psum_ready.get(nm)
                if pr is not None:
                    self.wait(eng, "sp", pr)
        for nm, is_out in names:
            if is_out:
                assert nm not in self.pe_pending_r or eng == "tensor", (
                    f"write to {nm} while a PE read is not yet fenced")
                for sem, c in self.readers.get(nm, {}).items():
                    self.wait(eng, sem, c)
        if inc is not None:
            self.cnt[inc] = self.cnt.get(inc, 0) + inc_n
            sem, n = self.sems[inc], inc_n

            def fn(e, _m=method, _a=args, _k=kw, _s=sem, _n=n):
                getattr(e, _m)(*_a, **_k).then_inc(_s, _n)
        else:
            def fn(e, _m=method, _a=args, _k=kw):
                getattr(e, _m)(*_a, **_k)
        self.q[eng].append(fn)
        if eng == "tensor":
            for nm, is_out in names:
                if is_out:
                    self.pe_pending_w.add(nm)
                else:
                    self.pe_pending_r.add(nm)
            if inc is not None:
                c = self.cnt[inc]
                for nm in self.pe_pending_w:
                    self.psum_ready[nm] = c
                self.pe_pending_w.clear()
                for nm in self.pe_pending_r:
                    self._mark(self.readers, nm, "sp", c)
                self.pe_pending_r.clear()
        elif eng in _ENGSEM:
            assert inc == _ENGSEM[eng], f"writer on {eng} must inc: {method}"
            c = self.cnt[inc]
            for nm, is_out in names:
                if is_out:
                    self._mark(self.dirty, nm, _ENGSEM[eng], c)
                else:
                    self._mark(self.readers, nm, _ENGSEM[eng], c)
        return self.cnt.get(inc, 0) if inc else None

    def wait(self, eng, sem, val=None):
        if val is None:
            val = self.cnt.get(sem, 0)
        if val <= 0 or self.waited[eng].get(sem, -1) >= val:
            return
        self.waited[eng][sem] = val
        h = self.sems[sem]
        self.q[eng].append(lambda e, _h=h, _v=val: e.wait_ge(_h, _v))


def build_program():
    nc = bass.Bass("TRN2", target_bir_lowering=False)
    n_own = NCH - OWN0
    TT_ = min(TT, n_own * CH)
    NTT = (n_own * CH) // TT_
    ntok = TT_ // 128

    xpad = nc.dram_tensor("xpad", [NCH * CH, D], F32, kind="ExternalInput").ap()
    wq = nc.dram_tensor("wq", [D, D], F32, kind="ExternalInput").ap()
    wk = nc.dram_tensor("wk", [D, D], F32, kind="ExternalInput").ap()
    wv = nc.dram_tensor("wv", [D, D], F32, kind="ExternalInput").ap()
    wout = nc.dram_tensor("wout", [D, D], F32, kind="ExternalInput").ap()
    wbeta = nc.dram_tensor("wbeta", [D, NH], F32, kind="ExternalInput").ap()
    bbeta = nc.dram_tensor("bbeta", [NH], F32, kind="ExternalInput").ap()
    wgate = nc.dram_tensor("wgate", [D, NE], F32, kind="ExternalInput").ap()
    wgu = nc.dram_tensor("wgu", [NE, D, 2 * FF], F32, kind="ExternalInput").ap()
    wdn = nc.dram_tensor("wdn", [NE, FF, D], F32, kind="ExternalInput").ap()
    cident = nc.dram_tensor("cident", [128, 128], F32, kind="ExternalInput").ap()
    cslmask = nc.dram_tensor("cslmask", [128, 128], F32, kind="ExternalInput").ap()
    csumask = nc.dram_tensor("csumask", [128, 128], F32, kind="ExternalInput").ap()
    out = nc.dram_tensor("out", [n_own * CH, D], F32, kind="ExternalOutput").ap()

    with ExitStack() as ctx:
        sb = lambda n, s, d: ctx.enter_context(nc.sbuf_tensor(n, s, d))
        psd = lambda n, s, d: ctx.enter_context(nc.psum_tensor(n, s, d))

        SHARED = 106816
        shared = sb("shared", [128, SHARED], mybir.dt.uint8)
        shared_base = nc.lookup_mloc(shared).addr
        _roff = [0]

        def _ralloc(name, shape, dtype):
            per = 1
            for d_ in shape[1:]:
                per *= d_
            per *= mybir.dt.size(dtype)
            h = nc.alloc_sbuf_tensor_at(name, shape, dtype,
                                        offset=shared_base + _roff[0])
            _roff[0] = (_roff[0] + per + 31) // 32 * 32
            assert _roff[0] <= SHARED, f"shared region overflow at {name}"
            return h

        xnT_own = sb("xnT_own", [128, NK * n_own * CH], BF16)
        acc = sb("acc", [128, n_own * D], F32)
        gates = sb("gates", [128, n_own * NE], F32)
        M32 = sb("M32", [64, NH * DH], F32)
        M16 = sb("M16", [64, NH * DH], BF16)
        identB = sb("identB", [128, 128], BF16)
        identF = sb("identF", [16, 16], F32)
        slmask = sb("slmask", [128, 128], BF16)
        sumask = sb("sumask", [128, 128], BF16)

        WqB = _ralloc("WqB", [128, NK * D], BF16)
        WkB = _ralloc("WkB", [128, NK * D], BF16)
        WvB = _ralloc("WvB", [128, NK * D], BF16)
        WoB = _ralloc("WoB", [128, NK * D], BF16)
        WbB = _ralloc("WbB", [128, NK * NH], BF16)
        WgB = _ralloc("WgB", [128, NK * NE], BF16)
        bbn = _ralloc("bbn", [16, 1], F32)
        bbp = _ralloc("bbp", [16, 1], F32)
        xin = _ralloc("xin0", [128, D], F32)
        xnb = _ralloc("xnb", [128, D], BF16)
        KT = _ralloc("KTb", [128, 8 * 128], BF16)
        VT = _ralloc("VTb", [128, 8 * 128], BF16)
        QT = _ralloc("QTb", [128, 8 * 128], BF16)
        betaT = _ralloc("betaT", [16, 128], F32)
        bcols = _ralloc("bcols", [128, 16], F32)
        bcolsn = _ralloc("bcolsn", [128, 16], F32)
        zgs = _ralloc("zgs", [8, 128], F32)
        zt = _ralloc("ztb", [128, 8], F32)
        Ktok = [_ralloc(f"Ktok{g}", [128, 8 * DH], BF16) for g in range(2)]
        Kn = Ktok
        Vtb = [_ralloc(f"Vtb{g}", [128, 8 * DH], BF16) for g in range(2)]
        Rt = Vtb
        KnT = [_ralloc(f"KnT{g}", [64, 8 * 128], BF16) for g in range(2)]
        ssb = [_ralloc(f"ssb{g}", [128, 8], F32) for g in range(2)]
        invb = [_ralloc(f"invb{g}", [128, 8], F32) for g in range(2)]
        Gb = [_ralloc(f"Gb{g}", [128, 8 * 128], BF16) for g in range(2)]
        Qh0 = Gb
        Gp = [_ralloc(f"Gp{g}", [128, 8 * 128], BF16) for g in range(2)]
        Pk = Gb
        Pq = Gp
        Sa = [_ralloc(f"Sa{g}", [128, 8 * 128], BF16) for g in range(2)]
        Xg = [_ralloc(f"Xg{g}", [128, 8 * DH], BF16) for g in range(2)]
        Lg = Sa
        Oall = [_ralloc(f"Oall{g}", [128, 8 * DH], BF16) for g in range(2)]
        Qtok = Oall
        OTb = _ralloc("OTb", [128, 8 * 128], BF16)
        c_s1 = _ralloc("c_s1", [128, 1], F32)
        c_mean = _ralloc("c_mean", [128, 1], F32)
        c_s2 = _ralloc("c_s2", [128, 1], F32)
        c_m2 = _ralloc("c_m2", [128, 1], F32)
        c_var = _ralloc("c_var", [128, 1], F32)
        c_rstd = _ralloc("c_rstd", [128, 1], F32)
        g_nm = _ralloc("g_nm", [128, 1], F32)
        g_ex = _ralloc("g_ex", [128, 8], F32)
        g_sum = _ralloc("g_sum", [128, 1], F32)
        g_rs = _ralloc("g_rs", [128, 1], F32)
        g_p = _ralloc("g_p", [128, 8], F32)
        g_m1 = _ralloc("g_m1", [128, 1], F32)
        g_m2 = _ralloc("g_m2", [128, 1], F32)
        g_mv = _ralloc("g_mv", [128, 8], F32)
        g_pm1 = _ralloc("g_pm1", [128, 8], F32)
        g_pm2 = _ralloc("g_pm2", [128, 8], F32)
        g_ne = _ralloc("g_ne", [128, 8], F32)
        g_den = _ralloc("g_den", [128, 1], F32)
        A_BYTES = _roff[0]

        _roff[0] = 0
        WguB = [_ralloc("WguB0", [128, NK * 1024], BF16),
                _ralloc("WguB1", [128, NK * 1024], BF16)]
        WdnB = [_ralloc("WdnB0", [128, NFC * 1024], BF16),
                _ralloc("WdnB1", [128, NFC * 1024], BF16)]
        actb = _ralloc("actb", [128, NFC * 512], BF16)
        sgb = [_ralloc("sgb0", [128, 512], F32), _ralloc("sgb1", [128, 512], F32)]

        psA = psd("psA", [128, 512], F32)
        psB = psd("psB", [128, 512], F32)
        psC = psd("psC", [128, 512], F32)
        psD = psd("psD", [128, 512], F32)
        psW = psd("psW", [128, 512], F32)
        psV = psd("psV", [128, 512], F32)
        psT = psd("psT", [128, 1024], BF16)
        psU = psd("psU", [128, 1024], BF16)
        fpair = [(psA, psB), (psC, psD)]
        trb = [psT, psU]
        kb = [psV, psW]

        sems = {}
        for name in ("sc", "si", "sxe", "sxo", "swp", "sw", "so",
                     "sp", "sv", "sa"):
            sems[name] = ctx.enter_context(nc.semaphore(name))
        S = Sched(sems)
        block = ctx.enter_context(nc.Block())
        op, wait = S.op, S.wait

        def vop(method, *args, **kw):
            op("vector", method, *args, inc="sv", **kw)

        def aop(method, *args, **kw):
            op("scalar", method, *args, inc="sa", **kw)

        def F32T(g, hl):
            bank = fpair[g][0] if hl < 4 else fpair[g][1]
            s_ = hl % 4
            return bank[:, s_ * 128:(s_ + 1) * 128]

        op("sync", "dma_start", out=identF[:], in_=cident[0:16, 0:16], inc="sc", inc_n=16)
        op("gpsimd", "dma_start", out=slmask[:], in_=cslmask, inc="swp", inc_n=16)
        op("gpsimd", "dma_start", out=sumask[:], in_=csumask, inc="swp", inc_n=16)
        op("sync", "dma_start", out=bbp[:],
           in_=bbeta.rearrange("(p o) -> p o", o=1), inc="sc", inc_n=16)
        op("sync", "dma_start",
           out=acc[:].rearrange("p (i d) -> p i d", d=D),
           in_=xpad[OWN0 * CH:NCH * CH, :].rearrange("(i p) d -> p i d", p=128),
           inc="si", inc_n=16)

        op("gpsimd", "dma_start", out=identB[:], in_=cident, inc="swp", inc_n=16)
        for wsrc, wdst in ((wq, WqB), (wk, WkB), (wv, WvB), (wout, WoB)):
            for k in range(NK):
                op("gpsimd", "dma_start", out=wdst[:, k * D:(k + 1) * D],
                   in_=wsrc[k * 128:(k + 1) * 128, :], inc="swp", inc_n=16)
        op("gpsimd", "dma_start",
           out=WbB[:].rearrange("p (k h) -> p k h", k=NK),
           in_=wbeta.rearrange("(k p) h -> p k h", p=128), inc="swp", inc_n=16)
        op("gpsimd", "dma_start",
           out=WgB[:].rearrange("p (k h) -> p k h", k=NK),
           in_=wgate.rearrange("(k p) h -> p k h", p=128), inc="swp", inc_n=16)
        proj_w = S.cnt["swp"]

        wait("vector", "sc")
        vop("tensor_scalar_mul", bbn[:], bbp[:], -1.0)
        vop("memset", M32[:], 0.0)
        vop("memset", M16[:], 0.0)

        wait("tensor", "sc")
        wait("tensor", "swp", proj_w)
        xnb_free = 0
        xin_free = 0
        for c in (range(NCH) if "A" in PHASES else []):
            own = c >= OWN0
            tc = c - OWN0
            sxp = "sxe" if c % 2 == 0 else "sxo"
            wait("sync", "sv", xin_free)
            op("sync", "dma_start", out=xin[:], in_=xpad[c * CH:(c + 1) * CH, :],
               inc=sxp, inc_n=16)

            wait("vector", sxp)
            wait("vector", "sp", xnb_free)
            vop("reduce_sum", c_s1[:], xin[:], axis=AX.X)
            vop("tensor_scalar_mul", c_mean[:], c_s1[:], 1.0 / D)
            vop("tensor_mul", xnb[:], xin[:], xin[:])
            vop("reduce_sum", c_s2[:], xnb[:], axis=AX.X)
            vop("tensor_mul", c_m2[:], c_mean[:], c_mean[:])
            vop("scalar_tensor_tensor", c_var[:], c_s2[:], 1.0 / D, c_m2[:],
                ALU.mult, ALU.subtract)
            vop("tensor_scalar_add", c_var[:], c_var[:], 1e-5)
            aop("activation", c_rstd[:], c_var[:], AF.Ln)
            aop("activation", c_rstd[:], c_rstd[:], AF.Exp, scale=-0.5)
            vop("tensor_scalar", xnb[:], xin[:], c_mean[:], c_rstd[:],
                ALU.subtract, ALU.mult)
            xin_free = S.cnt["sv"]

            for i in range(NK):
                op("tensor", "transpose", psT[:, i * 128:(i + 1) * 128],
                   in_=xnb[:, i * 128:(i + 1) * 128], identity=identB[:],
                   inc=("sp" if i == NK - 1 else None))
            xnb_free = S.cnt["sp"]
            slot = tc if own else 0
            xd = xnT_own[:].rearrange("p (k t) -> p k t", k=NK)[
                :, :, slot * CH:(slot + 1) * CH]
            vop("tensor_copy", xd, psT[:].rearrange("p (k t) -> p k t", k=NK))

            def XN(k, _slot=slot):
                base = k * (n_own * CH) + _slot * CH
                return xnT_own[:, base:base + CH]

            def proj(wbuf, dst):
                for half, bank in ((0, psA), (1, psB)):
                    for i in range(4):
                        ii = half * 4 + i
                        for k in range(NK):
                            op("tensor", "matmul", bank[:, i * 128:(i + 1) * 128],
                               lhsT=wbuf[:, k * D + ii * 128:k * D + (ii + 1) * 128],
                               rhs=XN(k), start=(k == 0), stop=(k == NK - 1),
                               inc=("sp" if (i == 3 and k == NK - 1) else None))
                    vop("tensor_copy", dst[:, half * 512:(half + 1) * 512], bank[:])

            proj(WkB, KT)
            proj(WvB, VT)
            if own:
                proj(WqB, QT)

            for k in range(NK):
                op("tensor", "matmul", psV[0:16, 0:128],
                   lhsT=WbB[:, k * NH:(k + 1) * NH], rhs=XN(k),
                   start=(k == 0), stop=(k == NK - 1),
                   inc=("sp" if k == NK - 1 else None))
            aop("activation", betaT[:], psV[0:16, 0:128], AF.Exp,
                scale=-1.0, bias=bbn[:])
            vop("tensor_scalar_add", betaT[:], betaT[:], 1.0)
            vop("reciprocal", betaT[:], betaT[:])
            op("tensor", "transpose", psV[:, 288:304], in_=betaT[:],
               identity=identF[:], inc="sp")
            vop("tensor_copy", bcols[:], psV[:, 288:304])
            vop("tensor_scalar_mul", bcolsn[:], bcols[:], -1.0)

            if own:
                for k in range(NK):
                    op("tensor", "matmul", psV[0:8, 128:256],
                       lhsT=WgB[:, k * NE:(k + 1) * NE], rhs=XN(k),
                       start=(k == 0), stop=(k == NK - 1),
                       inc=("sp" if k == NK - 1 else None))
                vop("tensor_copy", zgs[:], psV[0:8, 128:256])
                op("tensor", "transpose", psV[:, 304:312], in_=zgs[:],
                   identity=identF[0:8, 0:8], inc="sp")
                vop("tensor_copy", zt[:], psV[:, 304:312])
                vop("reduce_max", g_nm[:], zt[:], axis=AX.X, negate=True)
                aop("activation", g_ex[:], zt[:], AF.Exp, bias=g_nm[:])
                vop("reduce_sum", g_sum[:], g_ex[:], axis=AX.X)
                vop("reciprocal", g_rs[:], g_sum[:])
                vop("tensor_scalar_mul", g_p[:], g_ex[:], g_rs[:])
                vop("reduce_max", g_m1[:], g_p[:], axis=AX.X)
                vop("memset", g_mv[:], -1e30)
                vop("tensor_copy", g_mv[:, 0:1], g_m1[:])
                vop("match_replace", g_pm1[:], in_to_replace=g_mv[:],
                    in_values=g_p[:], imm_value=-1.0)
                vop("reduce_max", g_m2[:], g_pm1[:], axis=AX.X)
                vop("tensor_copy", g_mv[:, 0:1], g_m2[:])
                vop("match_replace", g_pm2[:], in_to_replace=g_mv[:],
                    in_values=g_pm1[:], imm_value=-1.0)
                vop("tensor_tensor", g_ne[:], g_p[:], g_pm2[:], ALU.not_equal)
                vop("tensor_add", g_den[:], g_m1[:], g_m2[:])
                vop("tensor_scalar_max", g_den[:], g_den[:], 1e-9)
                vop("reciprocal", g_den[:], g_den[:])
                vop("tensor_mul", g_ne[:], g_ne[:], g_p[:])
                vop("tensor_scalar", gates[:, tc * NE:(tc + 1) * NE],
                    g_ne[:], g_den[:], None, ALU.mult)

            def PTile(buf, g, p):
                t = g * 4 + p
                return buf[:, t * 128:(t + 1) * 128]

            def BC(g, hl):
                j = g * 8 + hl
                return bcols[:, j:j + 1]

            def BCN(g, hl):
                j = g * 8 + hl
                return bcolsn[:, j:j + 1]

            def s_t1(g):
                tr = trb[g]
                for p in range(4):
                    op("tensor", "transpose", tr[:, p * 128:(p + 1) * 128],
                       in_=PTile(KT, g, p), identity=identB[:],
                       inc=("sp" if p == 3 else None))
                vop("tensor_copy", Ktok[g][:], tr[:, 0:512])

            def s_t2(g):
                vop("tensor_mul", kb[g][:], Ktok[g][:], Ktok[g][:])
                vop("reduce_sum", ssb[g][:],
                    kb[g][:].rearrange("p (h d) -> p h d", h=8), axis=AX.X)
                vop("tensor_scalar_max", ssb[g][:], ssb[g][:], 1e-24)
                aop("activation", invb[g][:], ssb[g][:], AF.Ln)
                aop("activation", invb[g][:], invb[g][:], AF.Exp, scale=-0.5)
                for hl in range(8):
                    vop("tensor_scalar_mul", Kn[g][:, hl * DH:(hl + 1) * DH],
                        Ktok[g][:, hl * DH:(hl + 1) * DH], invb[g][:, hl:hl + 1])

            def s_t4(g):
                tr = trb[g]
                for hl in range(8):
                    op("tensor", "transpose", tr[0:64, hl * 128:(hl + 1) * 128],
                       in_=Kn[g][:, hl * DH:(hl + 1) * DH], identity=identB[:],
                       inc=("sp" if hl == 7 else None))
                vop("tensor_copy", KnT[g][:], tr[0:64, :])

            def s_t5a(g):
                tr = trb[g]
                for p in range(4):
                    op("tensor", "transpose", tr[:, p * 128:(p + 1) * 128],
                       in_=PTile(VT, g, p), identity=identB[:],
                       inc=("sp" if p == 3 else None))
                for hl in range(8):
                    pp, b = hl // 2, 64 * (hl % 2)
                    vop("tensor_scalar_mul", Vtb[g][:, hl * DH:(hl + 1) * DH],
                        tr[:, pp * 128 + b:pp * 128 + b + 64], BC(g, hl))

            def s_t5b(g):
                for hl in range(8):
                    hg = g * 8 + hl
                    op("tensor", "matmul", kb[g][:, hl * DH:(hl + 1) * DH],
                       lhsT=KnT[g][0:64, hl * 128:(hl + 1) * 128],
                       rhs=M16[0:64, hg * DH:(hg + 1) * DH],
                       start=True, stop=True, inc=("sp" if hl == 7 else None))
                for hl in range(8):
                    vop("scalar_tensor_tensor", Rt[g][:, hl * DH:(hl + 1) * DH],
                        kb[g][:, hl * DH:(hl + 1) * DH], BCN(g, hl),
                        Vtb[g][:, hl * DH:(hl + 1) * DH], ALU.mult, ALU.add)

            def s_t6(g):
                for hl in range(8):
                    op("tensor", "matmul", F32T(g, hl),
                       lhsT=KnT[g][0:64, hl * 128:(hl + 1) * 128],
                       rhs=KnT[g][0:64, hl * 128:(hl + 1) * 128],
                       start=True, stop=True, inc=("sp" if hl == 7 else None))
                for hl in range(8):
                    vop("scalar_tensor_tensor", Gb[g][:, hl * 128:(hl + 1) * 128],
                        F32T(g, hl), BC(g, hl), slmask[:], ALU.mult, ALU.mult)

            def s_t7(g):
                tr = trb[g]
                for hl in range(8):
                    op("tensor", "transpose", tr[:, hl * 128:(hl + 1) * 128],
                       in_=Gb[g][:, hl * 128:(hl + 1) * 128], identity=identB[:],
                       inc=("sp" if hl == 7 else None))
                vop("tensor_copy", Gp[g][:], tr[:])
                for hl in range(8):
                    vop("tensor_sub", Sa[g][:, hl * 128:(hl + 1) * 128],
                        identB[:], Gp[g][:, hl * 128:(hl + 1) * 128])

            def s_lev_p(g, lev):
                for hl in range(8):
                    sl_ = slice(hl * 128, (hl + 1) * 128)
                    op("tensor", "matmul", F32T(g, hl),
                       lhsT=Pq[g][:, sl_], rhs=Pk[g][:, sl_],
                       start=True, stop=True, inc=("sp" if hl == 7 else None))
                vop("tensor_copy", Pk[g][:, 0:512], fpair[g][0][:])
                vop("tensor_copy", Pk[g][:, 512:1024], fpair[g][1][:])

            def s_lev_pt(g, lev):
                tr = trb[g]
                for hl in range(8):
                    op("tensor", "transpose", tr[:, hl * 128:(hl + 1) * 128],
                       in_=Pk[g][:, hl * 128:(hl + 1) * 128], identity=identB[:],
                       inc=("sp" if hl == 7 else None))
                vop("tensor_copy", Pq[g][:], tr[:])

            def s_lev_s(g, lev):
                for hl in range(8):
                    sl_ = slice(hl * 128, (hl + 1) * 128)
                    op("tensor", "matmul", F32T(g, hl), lhsT=Pk[g][:, sl_],
                       rhs=Sa[g][:, sl_], start=True, stop=False)
                    op("tensor", "matmul", F32T(g, hl), lhsT=identB[:],
                       rhs=Sa[g][:, sl_], start=False, stop=True,
                       inc=("sp" if hl == 7 else None))
                vop("tensor_copy", Sa[g][:, 0:512], fpair[g][0][:])
                vop("tensor_copy", Sa[g][:, 512:1024], fpair[g][1][:])

            def s_t9(g):
                for hl in range(8):
                    op("tensor", "matmul", kb[g][:, hl * DH:(hl + 1) * DH],
                       lhsT=Sa[g][:, hl * 128:(hl + 1) * 128],
                       rhs=Rt[g][:, hl * DH:(hl + 1) * DH],
                       start=True, stop=True, inc=("sp" if hl == 7 else None))
                vop("tensor_copy", Xg[g][:], kb[g][:])

            def s_t10(g):
                for hl in range(8):
                    op("tensor", "matmul", kb[g][0:64, hl * DH:(hl + 1) * DH],
                       lhsT=Kn[g][:, hl * DH:(hl + 1) * DH],
                       rhs=Xg[g][:, hl * DH:(hl + 1) * DH],
                       start=True, stop=True, inc=("sp" if hl == 7 else None))
                for hl in range(8):
                    hg = g * 8 + hl
                    vop("tensor_tensor", M32[:, hg * DH:(hg + 1) * DH],
                        M32[:, hg * DH:(hg + 1) * DH],
                        kb[g][0:64, hl * DH:(hl + 1) * DH], ALU.add)

            def s_q(g):
                tr = trb[g]
                for p in range(4):
                    op("tensor", "transpose", tr[:, p * 128:(p + 1) * 128],
                       in_=PTile(QT, g, p), identity=identB[:],
                       inc=("sp" if p == 3 else None))
                vop("tensor_copy", Qtok[g][:], tr[:, 0:512])
                for hl in range(8):
                    op("tensor", "transpose", tr[0:64, hl * 128:(hl + 1) * 128],
                       in_=Qtok[g][:, hl * DH:(hl + 1) * DH], identity=identB[:],
                       inc=("sp" if hl == 7 else None))
                vop("tensor_copy", Qh0[g][0:64, :], tr[0:64, :])

            def s_kq(g):
                for hl in range(8):
                    op("tensor", "matmul", F32T(g, hl),
                       lhsT=KnT[g][0:64, hl * 128:(hl + 1) * 128],
                       rhs=Qh0[g][0:64, hl * 128:(hl + 1) * 128],
                       start=True, stop=True, inc=("sp" if hl == 7 else None))
                for hl in range(8):
                    vop("tensor_mul", Lg[g][:, hl * 128:(hl + 1) * 128],
                        F32T(g, hl), sumask[:])

            def s_o(g):
                for hl in range(8):
                    hg = g * 8 + hl
                    op("tensor", "matmul", kb[g][:, hl * DH:(hl + 1) * DH],
                       lhsT=Qh0[g][0:64, hl * 128:(hl + 1) * 128],
                       rhs=M16[0:64, hg * DH:(hg + 1) * DH],
                       start=True, stop=False)
                    op("tensor", "matmul", kb[g][:, hl * DH:(hl + 1) * DH],
                       lhsT=Lg[g][:, hl * 128:(hl + 1) * 128],
                       rhs=Xg[g][:, hl * DH:(hl + 1) * DH],
                       start=False, stop=True, inc=("sp" if hl == 7 else None))
                vop("tensor_copy", Oall[g][:], kb[g][:])

            def s_ot(g):
                tr = trb[g]
                for p in range(4):
                    op("tensor", "transpose", tr[:, p * 128:(p + 1) * 128],
                       in_=Oall[g][:, p * 128:(p + 1) * 128], identity=identB[:],
                       inc=("sp" if p == 3 else None))
                vop("tensor_copy", OTb[:, g * 512:(g + 1) * 512], tr[:, 0:512])

            steps = [s_t1, s_t2, s_t4, s_t5a, s_t5b, s_t6, s_t7]
            for lev in range(1, 7):
                steps.append(lambda g, _l=lev: s_lev_p(g, _l))
                if lev < 6:
                    steps.append(lambda g, _l=lev: s_lev_pt(g, _l))
                steps.append(lambda g, _l=lev: s_lev_s(g, _l))
            steps += [s_t9, s_t10]
            if own:
                steps += [s_q, s_kq, s_o, s_ot]
            for st in steps:
                st(0)
                st(1)

            if own:
                for i in range(8):
                    bank = psC if i < 4 else psD
                    for k in range(NK):
                        op("tensor", "matmul",
                           bank[:, (i % 4) * 128:((i % 4) + 1) * 128],
                           lhsT=WoB[:, k * D + i * 128:k * D + (i + 1) * 128],
                           rhs=OTb[:, k * 128:(k + 1) * 128],
                           start=(k == 0), stop=(k == NK - 1),
                           inc=("sp" if (i == 7 and k == NK - 1) else None))
                vop("tensor_copy", Lg[1][:, 0:512], psC[:])
                vop("tensor_copy", Lg[1][:, 512:1024], psD[:])
                for i in range(8):
                    op("tensor", "transpose", psT[:, i * 128:(i + 1) * 128],
                       in_=Lg[1][:, i * 128:(i + 1) * 128], identity=identB[:],
                       inc=("sp" if i == 7 else None))
                wait("vector", "si", 16)
                vop("tensor_tensor", acc[:, tc * D:(tc + 1) * D],
                    acc[:, tc * D:(tc + 1) * D], psT[:], ALU.add)

            vop("tensor_copy", M16[:], M32[:])

        wait("tensor", "sv")
        wait("tensor", "sa")
        pa_sp = S.cnt.get("sp", 0)
        wait("gpsimd", "sp", pa_sp)
        wait("gpsimd", "sv")
        wait("gpsimd", "sa")
        wgu5 = wgu.rearrange("e r (h f c) -> e f r h c", h=2, f=NFB)
        wdn4 = wdn.rearrange("e (f p) d -> e p f d", p=128)

        wg_free = [pa_sp, pa_sp]
        it = 0
        for e_i in range(NE if "B" in PHASES else 0):
            for fb in range(NFB):
                par = it % 2
                wait("gpsimd", "sp", wg_free[par])
                wait("gpsimd", "sw")
                for k in range(NK):
                    op("gpsimd", "dma_start",
                       out=WguB[par][:, k * 1024:(k + 1) * 1024].rearrange(
                           "p (h c) -> p h c", h=2),
                       in_=wgu5[e_i, fb, k * 128:(k + 1) * 128, :, :],
                       inc="sw", inc_n=16)
                op("gpsimd", "dma_start",
                   out=WdnB[par][:].rearrange("p (f d) -> p f d", f=NFC),
                   in_=wdn4[e_i, :, fb * NFC:(fb + 1) * NFC, :],
                   inc="sw", inc_n=16)
                w_cnt = S.cnt["sw"]

                wait("tensor", "sw", w_cnt)
                for tt in range(NTT):
                    for fc in range(NFC):
                        gbank = psA if fc % 2 == 0 else psC
                        ubank = psB if fc % 2 == 0 else psD
                        for k in range(NK):
                            op("tensor", "matmul", gbank[:, 0:TT_],
                               lhsT=WguB[par][:, k * 1024 + fc * 128:
                                              k * 1024 + (fc + 1) * 128],
                               rhs=xnT_own[:, k * (n_own * CH) + tt * TT_:
                                           k * (n_own * CH) + (tt + 1) * TT_],
                               start=(k == 0), stop=(k == NK - 1),
                               inc=("sp" if k == NK - 1 else None))
                        for k in range(NK):
                            op("tensor", "matmul", ubank[:, 0:TT_],
                               lhsT=WguB[par][:, k * 1024 + 512 + fc * 128:
                                              k * 1024 + 512 + (fc + 1) * 128],
                               rhs=xnT_own[:, k * (n_own * CH) + tt * TT_:
                                           k * (n_own * CH) + (tt + 1) * TT_],
                               start=(k == 0), stop=(k == NK - 1),
                               inc=("sp" if k == NK - 1 else None))
                        sg = sgb[fc % 2]
                        aop("activation", sg[:, 0:TT_], gbank[:, 0:TT_], AF.Silu)
                        vop("tensor_mul", actb[:, fc * TT_:(fc + 1) * TT_],
                            sg[:, 0:TT_], ubank[:, 0:TT_])
                    for gi in range(2 * ntok):
                        tok, dh = gi // 2, gi % 2
                        dbank = psW if gi % 2 == 0 else psV
                        for fc in range(NFC):
                            op("tensor", "matmul", dbank[:],
                               lhsT=actb[:, fc * TT_ + tok * 128:
                                         fc * TT_ + (tok + 1) * 128],
                               rhs=WdnB[par][:, fc * 1024 + dh * 512:
                                             fc * 1024 + (dh + 1) * 512],
                               start=(fc == 0), stop=(fc == NFC - 1),
                               inc=("sp" if fc == NFC - 1 else None))
                        tc2 = tt * ntok + tok
                        vop("scalar_tensor_tensor",
                            acc[:, tc2 * D + dh * 512:tc2 * D + (dh + 1) * 512],
                            dbank[:], gates[:, tc2 * NE + e_i:tc2 * NE + e_i + 1],
                            acc[:, tc2 * D + dh * 512:tc2 * D + (dh + 1) * 512],
                            ALU.mult, ALU.add)
                wg_free[par] = S.cnt["sp"]
                it += 1

        wait("sync", "sv")
        op("sync", "dma_start", out=out.rearrange("(i p) d -> p i d", p=128),
           in_=acc[:].rearrange("p (i d) -> p i d", d=D), inc="so", inc_n=16)
        wait("sync", "so")
        wait("gpsimd", "so")

        @block.sync
        def _(e):
            for fn in S.q["sync"]:
                fn(e)

        @block.gpsimd
        def _(e):
            for fn in S.q["gpsimd"]:
                fn(e)

        @block.tensor
        def _(e):
            for fn in S.q["tensor"]:
                fn(e)

        @block.vector
        def _(e):
            for fn in S.q["vector"]:
                fn(e)

        @block.scalar
        def _(e):
            for fn in S.q["scalar"]:
                fn(e)

    return nc


_NC_CACHE = None


def kernel(L0_attn_x, L0_attn_Wq, L0_attn_Wk, L0_attn_Wv, L0_attn_Wbeta,
           L0_attn_b_beta, L0_attn_Wout, L0_ffn_w_gate, L0_ffn_w_gate_up,
           L0_ffn_w_down):
    global _NC_CACHE
    x = np.ascontiguousarray(np.asarray(L0_attn_x, dtype=np.float32))
    common = dict(
        wq=np.ascontiguousarray(L0_attn_Wq, dtype=np.float32),
        wk=np.ascontiguousarray(L0_attn_Wk, dtype=np.float32),
        wv=np.ascontiguousarray(L0_attn_Wv, dtype=np.float32),
        wout=np.ascontiguousarray(L0_attn_Wout, dtype=np.float32),
        wbeta=np.ascontiguousarray(L0_attn_Wbeta, dtype=np.float32),
        bbeta=np.ascontiguousarray(L0_attn_b_beta, dtype=np.float32),
        wgate=np.ascontiguousarray(L0_ffn_w_gate, dtype=np.float32),
        wgu=np.ascontiguousarray(L0_ffn_w_gate_up, dtype=np.float32),
        wdn=np.ascontiguousarray(L0_ffn_w_down, dtype=np.float32),
        cident=np.eye(128, dtype=np.float32),
        cslmask=np.tril(np.ones((128, 128), np.float32), -1),
        csumask=np.triu(np.ones((128, 128), np.float32), 0),
    )
    in_maps = []
    for c in range(8):
        b, h = c >> 1, c & 1
        if h == 0:
            xs = np.concatenate([np.zeros((2048, D), np.float32), x[b, 0:2048]], 0)
        else:
            xs = x[b]
        m = dict(common)
        m["xpad"] = np.ascontiguousarray(xs)
        in_maps.append(m)

    if _NC_CACHE is None:
        _NC_CACHE = build_program()
    res = run_bass_kernel_spmd(_NC_CACHE, in_maps, core_ids=list(range(8)))
    outp = np.empty((4, 4096, D), dtype=np.float32)
    for c in range(8):
        b, h = c >> 1, c & 1
        outp[b, h * 2048:(h + 1) * 2048, :] = res.results[c]["out"]
    return outp
